# version 17
# baseline (speedup 1.0000x reference)
import os
import numpy as np
import concourse.bass as bass
import concourse.mybir as mybir
from concourse.bass_utils import run_bass_kernel_spmd

F32 = mybir.dt.float32
BF16 = mybir.dt.bfloat16
I32 = mybir.dt.int32
AF = mybir.ActivationFunctionType
ALU = mybir.AluOpType
AP = bass.AP

NCORES = 8
D = 1024
SEQ = 16384
TOK = SEQ // NCORES
T = TOK // 128
G = T // 4
KC = D // 128
C_MAGIC = 12582912.0
EPS = 1e-5
PHASE = int(os.environ.get("ATH_PHASE", "3"))
PLANES = [0, 2, 4, 6, 1, 3, 5, 7]


def _register_ops():
    import concourse.dve_ops as dops
    if "ATH_RND" in dops.CUSTOM_DVE_REFS:
        globals()["_OP_RND"] = next(o for o in dops.OPS if o.name == "ATH_RND")
        globals()["_OP_AFF"] = next(o for o in dops.OPS if o.name == "ATH_AFFMAX")
        return
    from concourse.dve_op import DveOp
    from concourse.dve_spec import Spec, Src0, C0, C1, C2, Zero, maxx, lower
    from concourse.dve_uop import AluOp as UAluOp

    spec_rnd = Spec(body=((Src0 * C0 + C1) + C2) - C2)
    sha_rnd = {v: lower(spec_rnd, ver=v, perf_en=False, name="ATH_RND").sha(v)
               for v in ("v3", "v4")}
    op_rnd = DveOp("ATH_RND", spec_rnd, subdim=False, uops_sha=sha_rnd,
                   reference=lambda in0, in1, s0, s1, imm2: np.rint(in0 * s0 + s1))

    _aff = Src0 * C0 + C1
    spec_aff = Spec(body=maxx(_aff, Zero - _aff), accum=UAluOp.MAX, accum_init=Zero)
    sha_aff = {v: lower(spec_aff, ver=v, perf_en=False, name="ATH_AFFMAX").sha(v)
               for v in ("v3", "v4")}
    def _ref_aff(in0, in1, s0, s1, imm2):
        r = np.abs(in0 * s0 + s1)
        return r, np.max(r, axis=-1, keepdims=True)
    op_aff = DveOp("ATH_AFFMAX", spec_aff, subdim=False, uops_sha=sha_aff,
                   reference=_ref_aff)

    for op in (op_rnd, op_aff):
        dops.OPS.append(op)
        dops.CUSTOM_DVE_SPECS[op.name] = op.spec
        dops.CUSTOM_DVE_REFS[op.name] = op.ref
        dops._SUB_OPCODE_FOR_NAME[op.name] = dops._CUSTOM_DVE_ROW_BASE + len(dops.OPS) - 1
    globals()["_OP_RND"] = op_rnd
    globals()["_OP_AFF"] = op_aff


class _Ser:
    def __init__(self, eng, sem, mstones, counting):
        self.eng, self.sem, self.m, self.counting = eng, sem, mstones, counting
        self.n = 0

    def op(self, name, *args, key=None, **kw):
        if not self.counting:
            if self.n:
                self.eng.wait_ge(self.sem, self.n)
            getattr(self.eng, name)(*args, **kw).then_inc(self.sem, 1)
        self.n += 1
        if key is not None:
            self.m[key] = self.n

    def wait(self, sem, val):
        if not self.counting:
            self.eng.wait_ge(sem, val)


def build_nc():
    _register_ops()
    op_rnd, op_aff = globals()["_OP_RND"], globals()["_OP_AFF"]

    nc = bass.Bass("TRN2", target_bir_lowering=False)

    x_in = nc.dram_tensor("x", [TOK, D], F32, kind="ExternalInput").ap()
    wq_d = nc.dram_tensor("wq", [D, D], F32, kind="ExternalInput").ap()
    wk_d = nc.dram_tensor("wk", [D, D], F32, kind="ExternalInput").ap()
    wv_d = nc.dram_tensor("wv", [D, D], F32, kind="ExternalInput").ap()
    bq_d = nc.dram_tensor("bq", [D], F32, kind="ExternalInput").ap()
    bk_d = nc.dram_tensor("bk", [D], F32, kind="ExternalInput").ap()
    bv_d = nc.dram_tensor("bv", [D], F32, kind="ExternalInput").ap()
    pk_d = nc.dram_tensor("pkw", [D * D // 4], I32, kind="ExternalInput").ap()
    ws_d = nc.dram_tensor("wsc", [1], F32, kind="ExternalInput").ap()
    bb_d = nc.dram_tensor("fbias", [D], F32, kind="ExternalInput").ap()
    out_d = nc.dram_tensor("out", [TOK, D], F32, kind="ExternalOutput").ap()
    blo_scr = nc.dram_tensor("bloscr", [D], BF16).ap()

    from contextlib import ExitStack
    es = ExitStack()
    with es:
        def sb(name, shape, dt):
            return es.enter_context(nc.sbuf_tensor(name, shape, dt))
        def sem(name):
            return es.enter_context(nc.semaphore(name))

        xs = sb("xs", [128, T * D], F32)
        wqs = sb("wqs", [128, KC * D], BF16)
        wks = sb("wks", [128, KC * D], BF16)
        wvs = sb("wvs", [128, KC * D], BF16)
        wt = sb("wt", [128, KC * D], BF16)
        y2T = [sb(f"y2T{i}", [128, KC * 512], BF16) for i in range(2)]
        qf = sb("qf", [128, 8 * 512], BF16)
        kf = sb("kf", [128, 8 * 512], BF16)
        vf = sb("vf", [128, 8 * 512], BF16)
        vtp = [sb(f"vtp{i}", [128, D], BF16) for i in range(2)]
        rbf = sb("rbf", [128, D], F32)
        r1x = [sb(f"r1x{i}", [128, 1152], BF16) for i in range(2)]
        r1T = [sb(f"r1T{i}", [128, 1152], BF16) for i in range(2)]
        aT = [sb(f"aT{i}", [128, D], BF16) for i in range(2)]
        eT = [sb(f"eT{i}", [128, D], BF16) for i in range(2)]
        aTn = sb("aTn", [128, D], BF16)
        rb = sb("rb", [128, D], BF16)
        bias2 = sb("bias2", [128, D], BF16)
        pkr = sb("pkr", [128, D], I32)
        pkh = sb("pkh", [128, D // 2], I32)
        wtrh = [rb, aTn]
        bcols = sb("bcols", [128, 24], F32)
        bq8 = sb("bq8", [128, 8], F32)
        blockones = sb("blockones", [128, 128], BF16)
        identb = sb("identb", [128, 128], BF16)
        onesrow = sb("onesrow", [1, 128], F32)
        wssb = sb("wssb", [1, 1], F32)
        brows = sb("brows", [24, 128], F32)
        identf = sb("identf", [24, 24], F32)
        sums2 = sb("sums2", [128, 2 * T], F32)
        bnst = sb("bnst", [128, 12], F32)
        mvh = sb("mvh", [128, 4], F32)
        var_c = sb("var_c", [128, 2], F32)
        t1_c = sb("t1_c", [128, 1], F32)
        t2_c = sb("t2_c", [128, 1], F32)
        ta_c = sb("ta_c", [128, 1], F32)
        tb_c = sb("tb_c", [128, 1], F32)
        rstd_c = sb("rstd_c", [128, 4], F32)
        mneg_c = sb("mneg_c", [128, 2], F32)
        bln_c = sb("bln_c", [128, 2], F32)
        bA_c = sb("bA_c", [128, 1], F32)
        amax_c = sb("amax_c", [128, 1], F32)
        g_col = sb("g_col", [128, 1], F32)
        h_col = sb("h_col", [128, 1], F32)
        c_all = sb("c_all", [128, T], F32)
        wscol = sb("wscol", [128, 1], F32)
        wsr = sb("wsr", [128, 1], F32)
        wsc2 = sb("wsc2", [128, 1], F32)
        eps_c = sb("eps_c", [128, 1], F32)
        tiny_c = sb("tiny_c", [128, 1], F32)

        P0 = es.enter_context(nc.psum_tensor("P0", [128, D], F32))
        P1 = es.enter_context(nc.psum_tensor("P1", [128, D], F32))
        P2 = es.enter_context(nc.psum_tensor("P2", [128, D], F32))
        P3 = es.enter_context(nc.psum_tensor("P3", [128, D], F32))
        PF = [P0, P1]

        s_g = sem("s_g"); s_out = sem("s_out")
        s_lx4 = [sem(f"s_lx{j}") for j in range(4)]
        s_lds = sem("s_lds"); s_lpk = sem("s_lpk")
        s_lws = sem("s_lws")
        s_lb = sem("s_lb"); s_lw = sem("s_lw")
        s_vo = sem("s_vo"); s_ao = sem("s_ao")
        s_peinit = sem("s_peinit")
        s_tpw = sem("s_tpw")
        s_tpr = [sem(f"s_tpr{i}") for i in range(2)]
        s_tpy = [sem(f"s_tpy{i}") for i in range(2)]
        s_tpv = [sem(f"s_tpv{i}") for i in range(2)]
        s_ffn1 = sem("s_ffn1"); s_qkv = sem("s_qkv")
        s_st = sem("s_st"); s_stb = sem("s_stb"); s_bc = sem("s_bc"); s_av = sem("s_av")
        s_ffn3 = sem("s_ffn3")

        VM, AM = {}, {}

        def dve_body(S):
            S.op("tensor_scalar", identb[:], identb[:], 0.0, None, ALU.is_equal,
                 key="identb")
            S.op("tensor_scalar", identf[:], identf[:], 0.0, None, ALU.is_equal,
                 key="identf")
            for k, i in enumerate(PLANES):
                S.wait(s_lpk, 16 if k < 4 else 32)
                if k >= 2:
                    S.wait(s_tpw, k - 1)
                for h in range(2):
                    pkey = "pk0done" if (k == 3 and h == 1) else None
                    seg = slice(h * 512, (h + 1) * 512)
                    if i // 2 == 0:
                        S.op("tensor_scalar", pkh[:], pkr[:, seg], 3, None,
                             ALU.bitwise_and, key=pkey)
                    else:
                        S.op("tensor_scalar", pkh[:], pkr[:, seg], 2 * (i // 2), 3,
                             ALU.logical_shift_right, ALU.bitwise_and, key=pkey)
                    S.op("tensor_scalar", wtrh[k % 2][:, seg], pkh[:], 1.0, None,
                         ALU.subtract, key=("unpk", i) if h == 1 else None)
                if k == 0:
                    S.wait(s_lds, 96)
                    S.op("tensor_tensor", aT[0][0:1, :], rbf[0:1, :],
                         bias2[0:1, :], ALU.subtract, key="blo")
            S.wait(s_ao, AM.get("wscol", 0))
            S.op("reciprocal", wsr[:], wscol[:])
            S.op("tensor_scalar", wsc2[:], wscol[:], 0.5 / 127.0, None, ALU.mult)

            def stats(tile, l3):
                par = tile % 2
                xt = xs[:, tile * D:(tile + 1) * D]
                if not l3:
                    S.wait(s_lx4[tile // 4], 16)
                if tile >= 2:
                    S.wait(s_ao, AM.get(("rstd3" if l3 else "rstd1", tile - 2), 0))
                S.op("bn_stats", bnst[:, 0:6], xt[:, 0:512])
                S.op("bn_stats", bnst[:, 6:12], xt[:, 512:1024])
                S.op("bn_aggr", mvh[:, 0:2], bnst[:, 0:6])
                S.op("bn_aggr", mvh[:, 2:4], bnst[:, 6:12])
                S.op("tensor_tensor", t1_c[:], mvh[:, 0:1], mvh[:, 2:3], ALU.add)
                S.op("tensor_scalar", mneg_c[:, par:par + 1], t1_c[:], -0.5,
                     None, ALU.mult)
                S.op("tensor_tensor", t1_c[:], mvh[:, 0:1], mvh[:, 2:3], ALU.subtract)
                S.op("tensor_tensor", t1_c[:], t1_c[:], t1_c[:], ALU.mult)
                S.op("tensor_tensor", t2_c[:], mvh[:, 1:2], mvh[:, 3:4], ALU.add)
                S.op("tensor_scalar", t2_c[:], t2_c[:], 0.5, None, ALU.mult)
                S.op("scalar_tensor_tensor", var_c[:, par:par + 1], t1_c[:], 0.25,
                     t2_c[:], ALU.mult, ALU.add, key=("stat3" if l3 else "stat1", tile))

            def quant_ffn(tile, l3):
                par = tile % 2
                g = (T if l3 else 0) + tile
                if g >= 2:
                    S.wait(s_tpr[g % 2], 16 * (g // 2))
                S.wait(s_ao, AM.get(("rstd3" if l3 else "rstd1", tile), 0))
                rsc = rstd_c[:, par:par + 1]
                blc = bln_c[:, par:par + 1]
                S.op("_custom_dve", op_aff, out=aTn[:],
                     in0=xs[:, tile * D:(tile + 1) * D],
                     s0=rsc, s1=blc, accum_out=amax_c[:])
                S.op("tensor_scalar", amax_c[:], amax_c[:], 1e-5, None, ALU.max)
                S.op("reciprocal", t1_c[:], amax_c[:])
                S.op("tensor_scalar", g_col[:], t1_c[:], 127.0, rsc,
                     ALU.mult, ALU.mult)
                S.op("tensor_scalar", h_col[:], t1_c[:], 127.0, blc,
                     ALU.mult, ALU.mult, key=("qh", l3, tile))
                S.op("tensor_tensor", c_all[:, tile:tile + 1], amax_c[:], wsc2[:], ALU.mult)
                S.op("tensor_scalar", t2_c[:], t1_c[:], 127.0, wsr[:],
                     ALU.mult, ALU.mult)
                S.op("tensor_copy", r1x[par][:, 1024:1025], t2_c[:])
                S.op("tensor_tensor", r1x[par][:, 1025:1026], t2_c[:],
                     r1x[par][:, 1024:1025], ALU.subtract)
                S.op("tensor_copy", r1x[par][:, 1026:1027], r1x[par][:, 1024:1025])
                S.op("_custom_dve", op_rnd, out=r1x[par][:, 0:1024],
                     in0=xs[:, tile * D:(tile + 1) * D],
                     s0=g_col[:], s1=h_col[:], imm2=C_MAGIC,
                     key=("rnd3" if l3 else "rnd1", tile))

            def epi_ffn(tile, l3):
                S.wait(s_ffn3 if l3 else s_ffn1, tile + 1)
                S.op("scalar_tensor_tensor",
                     xs[:, tile * D:(tile + 1) * D], PF[tile % 2][:],
                     c_all[:, tile:tile + 1], xs[:, tile * D:(tile + 1) * D],
                     ALU.mult, ALU.add, key=("epi3" if l3 else "epi1", tile))

            def ffn_layer(l3, t0=0):
                if t0 == 0:
                    stats(0, l3)
                for t in range(T):
                    if t + 1 < T and t + 1 >= max(t0, 1):
                        stats(t + 1, l3)
                    if t >= t0:
                        quant_ffn(t, l3)
                    if t >= 1:
                        epi_ffn(t - 1, l3)
                epi_ffn(T - 1, l3)

            def evacs(g2):
                for i in range(24):
                    proj, oc = divmod(i, 8)
                    S.wait(s_qkv, g2 * 24 + i + 1)
                    if i == 0 and g2 >= 1:
                        S.wait(s_st, 4 * g2)
                        S.wait(s_tpv[0], 16 * g2 * 2)
                        S.wait(s_tpv[1], 16 * g2 * 2)
                    ph = (i % 2) * 512
                    if proj == 0:
                        S.op("tensor_scalar", qf[:, oc * 512:(oc + 1) * 512],
                             P2[:, ph:ph + 512], 0.125, bq8[:, oc:oc + 1],
                             ALU.mult, ALU.add, key=("evac", g2, i))
                    elif proj == 1:
                        S.op("tensor_scalar", kf[:, oc * 512:(oc + 1) * 512],
                             P2[:, ph:ph + 512], bcols[:, 8 + oc:9 + oc], None,
                             ALU.add, key=("evac", g2, i))
                    else:
                        for tl in range(4):
                            S.op("tensor_scalar",
                                 vf[:, tl * 1024 + oc * 128:tl * 1024 + (oc + 1) * 128],
                                 P2[:, ph + tl * 128:ph + (tl + 1) * 128],
                                 bcols[:, 16 + oc:17 + oc], None, ALU.add,
                                 key=("evac", g2, i) if tl == 3 else None)

            if PHASE < 1:
                return
            S.wait(s_tpw, 8)
            ffn_layer(False)
            if PHASE < 2:
                return
            evacs(0)
            for g2 in range(G):
                for tau in range(g2 * 4, g2 * 4 + 4):
                    S.wait(s_ao, AM.get(("rcp", tau), 0))
                    S.op("tensor_tensor", aTn[:], eT[tau % 2][:], rb[:],
                         ALU.mult, key=("atn", tau))
                    S.wait(s_av, tau + 1)
                    S.op("tensor_tensor", xs[:, tau * D:(tau + 1) * D],
                         xs[:, tau * D:(tau + 1) * D], PF[tau % 2][:], ALU.add,
                         key=("res", tau))
                    if PHASE >= 3 and g2 == 0:
                        if tau == 1:
                            stats(0, True)
                        elif tau == 2:
                            stats(1, True)
                        elif tau == 3:
                            quant_ffn(0, True)
                if g2 + 1 < G:
                    evacs(g2 + 1)
                if PHASE >= 3 and g2 == 0:
                    quant_ffn(1, True)
            if PHASE < 3:
                return
            ffn_layer(True, t0=2)

        def act_body(S):
            S.wait(s_peinit, 2)
            S.op("copy", bcols[:], P2[:, 0:24])
            S.op("activation", bq8[:], bcols[:, 0:8], AF.Identity, scale=0.125)
            S.op("copy", wscol[:], P3[:, 0:1], key="wscol")
            for k, i in enumerate(PLANES):
                S.wait(s_tpw, k + 1)
                S.op("copy", wt[:, i * D:(i + 1) * D], (P2, P3)[k % 2][:],
                     key=("wtevac", i))

            def rstdblk(t, l3):
                par = t % 2
                S.wait(s_vo, VM.get(("stat3" if l3 else "stat1", t), 0))
                if t >= 2:
                    S.wait(s_vo, VM.get(("qh", l3, t - 2), 0))
                S.op("activation", ta_c[:], var_c[:, par:par + 1], AF.Ln,
                     bias=eps_c[:], scale=1.0)
                S.op("activation", rstd_c[:, par:par + 1], ta_c[:], AF.Exp,
                     scale=-0.5)
                S.op("activation", bln_c[:, par:par + 1], mneg_c[:, par:par + 1],
                     AF.Identity, scale=rstd_c[:, par:par + 1],
                     key=("rstd3" if l3 else "rstd1", t))

            def accum2(j):
                xt = xs[:, j * D:(j + 1) * D]
                S.wait(s_vo, VM[("epi1", j)])
                S.op("activation", eT[0][:], xt, AF.Identity,
                     accum_out=sums2[:, 2 * j:2 * j + 1])
                S.op("activation", eT[1][:], xt, AF.Square,
                     accum_out=sums2[:, 2 * j + 1:2 * j + 2], key=("st2", j))

            def y2tile(t):
                    c = 2 + t % 2
                    S.op("activation", tb_c[:], sums2[:, 2 * t:2 * t + 1],
                         AF.Identity, scale=-1.0 / D)
                    S.op("activation", bA_c[:], tb_c[:], AF.Square)
                    S.op("activation", ta_c[:], bA_c[:], AF.Identity, scale=-1.0,
                         bias=eps_c[:])
                    S.op("activation", bA_c[:], sums2[:, 2 * t + 1:2 * t + 2],
                         AF.Ln, scale=1.0 / D, bias=ta_c[:])
                    S.op("activation", rstd_c[:, c:c + 1], bA_c[:], AF.Exp,
                         scale=-0.5, key=("rstd2", t))
                    S.op("activation", bA_c[:], tb_c[:], AF.Identity,
                         scale=rstd_c[:, c:c + 1])
                    if t == 0:
                        S.wait(s_vo, VM["blo"])
                        S.wait(s_lb, 16)
                    if t >= 2:
                        S.wait(s_tpy[t % 2], 16 * (t // 2))
                    S.op("activation", aT[t % 2][:], xs[:, t * D:(t + 1) * D],
                         AF.Identity, bias=bA_c[:], scale=rstd_c[:, c:c + 1],
                         key=("y2", t))

            def y2blk(g2, lo=0, hi=4):
                for t in range(g2 * 4 + lo, g2 * 4 + hi):
                    y2tile(t)

            def exps(g2, lo, hi):
                for tau in range(g2 * 4 + lo, g2 * 4 + hi):
                    S.wait(s_st, tau + 1)
                    if tau >= 2:
                        S.wait(s_bc, tau - 1)
                        S.wait(s_vo, VM.get(("atn", tau - 2), 0))
                    S.op("activation", eT[tau % 2][:], PF[tau % 2][:], AF.Exp,
                         key=("exp", tau))

            def rcps(g2, lo, hi):
                for tau in range(g2 * 4 + lo, g2 * 4 + hi):
                    S.wait(s_bc, tau + 1)
                    S.op("activation", rbf[:], P3[:], AF.Ln, bias=tiny_c[:],
                         scale=1.0, key=("ln", tau))
                    if tau >= 1:
                        S.wait(s_vo, VM.get(("atn", tau - 1), 0))
                    S.op("activation", rb[:], rbf[:], AF.Exp, scale=-1.0,
                         key=("rcp", tau))

            if PHASE < 1:
                return
            for t in range(T):
                rstdblk(t, False)
                if PHASE >= 2 and t >= 4:
                    accum2(t - 4)
                if PHASE >= 2 and 8 <= t <= 11:
                    y2tile(t - 8)
            if PHASE < 2:
                return
            for j in range(12, 16):
                accum2(j)
            for g2 in range(G):
                exps(g2, 0, 2)
                rcps(g2, 0, 2)
                if g2 + 1 < G:
                    y2blk(g2 + 1, 0, 2)
                exps(g2, 2, 4)
                if PHASE >= 3 and g2 == 0:
                    rstdblk(0, True)
                if g2 + 1 < G:
                    y2blk(g2 + 1, 2, 4)
                rcps(g2, 2, 4)
                if PHASE >= 3 and g2 == 0:
                    rstdblk(1, True)
            if PHASE < 3:
                return
            for t in range(2, T):
                rstdblk(t, True)

        for _ in range(2):
            dve_body(_Ser(None, None, VM, True))
            act_body(_Ser(None, None, AM, True))

        block = es.enter_context(nc.Block())

        @block.gpsimd
        def _(g):
            g.iota(identb[:], [[1, 128]], channel_multiplier=-1,
                   allow_small_or_imprecise_dtypes=True).then_inc(s_g, 1)
            g.iota(identf[:], [[1, 24]], channel_multiplier=-1,
                   allow_small_or_imprecise_dtypes=True).then_inc(s_g, 1)
            g.memzero(bias2[:]).then_inc(s_g, 1)
            g.memset(r1x[0][:, 1027:1152], 0).then_inc(s_g, 1)
            g.memset(r1x[1][:, 1027:1152], 0).then_inc(s_g, 1)
            g.memset(blockones[0:64, 0:64], 1.0).then_inc(s_g, 1)
            g.memset(blockones[0:64, 64:128], 0).then_inc(s_g, 1)
            g.memset(blockones[64:128, 0:64], 0).then_inc(s_g, 1)
            g.memset(blockones[64:128, 64:128], 1.0).then_inc(s_g, 1)
            g.memset(onesrow[:], 1.0).then_inc(s_g, 1)
            g.memset(eps_c[:], EPS).then_inc(s_g, 1)
            g.memset(tiny_c[:], 1e-35).then_inc(s_g, 1)
            g.wait_ge(s_g, 12)
            g.dma_start(pkr[:], pk_d.rearrange("(gg p n) -> gg p n", gg=2, p=128)[0]).then_inc(s_lpk, 16)
            g.dma_start(wssb[:], ws_d.rearrange("(a o) -> a o", a=1)).then_inc(s_lws, 16)
            g.dma_start(bias2[0:1, :], bb_d.rearrange("(a o) -> a o", a=1)).then_inc(s_lds, 16)
            g.dma_start(bias2[1:2, :], bb_d.rearrange("(a o) -> a o", a=1)).then_inc(s_lds, 16)
            g.dma_start(brows[0:8, :], bq_d.rearrange("(c p) -> c p", p=128)).then_inc(s_lds, 16)
            g.dma_start(brows[8:16, :], bk_d.rearrange("(c p) -> c p", p=128)).then_inc(s_lds, 16)
            g.dma_start(brows[16:24, :], bv_d.rearrange("(c p) -> c p", p=128)).then_inc(s_lds, 16)
            g.dma_start(rbf[0:1, :], bb_d.rearrange("(a o) -> a o", a=1)).then_inc(s_lds, 16)
            g.wait_ge(s_vo, VM["pk0done"])
            g.dma_start(pkr[:], pk_d.rearrange("(gg p n) -> gg p n", gg=2, p=128)[1]).then_inc(s_lpk, 16)
            g.wait_ge(s_vo, VM["blo"])
            g.dma_start(blo_scr.rearrange("(a o) -> a o", a=1), aT[0][0:1, :]).then_inc(s_lb, 16)
            g.wait_ge(s_lb, 16)
            g.dma_start(bias2[2:3, :], blo_scr.rearrange("(a o) -> a o", a=1)).then_inc(s_lb, 16)
            for wdst, wsrc in ((wqs, wq_d), (wks, wk_d), (wvs, wv_d)):
                for c in range(KC):
                    g.dma_start(wdst[:, c * D:(c + 1) * D],
                                wsrc[c * 128:(c + 1) * 128, :]).then_inc(s_lw, 16)
            for t in range(T):
                if PHASE >= 3:
                    g.wait_ge(s_vo, VM[("epi3", t)])
                elif PHASE == 2:
                    g.wait_ge(s_vo, VM[("res", t)])
                elif PHASE == 1:
                    g.wait_ge(s_vo, VM[("epi1", t)])
                else:
                    g.wait_ge(s_lx4[t // 4], 16)
                g.dma_start(out_d[t * 128:(t + 1) * 128, :],
                            xs[:, t * D:(t + 1) * D]).then_inc(s_out, 16)
            g.wait_ge(s_out, 16 * T)

        @block.scalar
        def _(a):
            for j in range(4):
                a.dma_start(xs[:, j * 4 * D:(j + 1) * 4 * D].rearrange("p (t d) -> p t d", t=4),
                            x_in[j * 512:(j + 1) * 512].rearrange("(t p) d -> p t d", p=128)
                            ).then_inc(s_lx4[j], 16)
            act_body(_Ser(a, s_ao, AM, False))

        @block.vector
        def _(v):
            v.wait_ge(s_g, 12)
            dve_body(_Ser(v, s_vo, VM, False))

        @block.tensor
        def _(pe):
            pe.wait_ge(s_g, 12)
            pe.wait_ge(s_vo, VM["identf"])
            pe.wait_ge(s_lws, 16)
            pe.wait_ge(s_lds, 96)
            pe.transpose(P2[:, 0:24], brows[0:24, :], identf[0:24, 0:24]).then_inc(s_peinit, 1)
            pe.matmul(P3[:, 0:1], onesrow[0:1, :], wssb[0:1, 0:1],
                      start=True, stop=True).then_inc(s_peinit, 1)
            for k, i in enumerate(PLANES):
                pe.wait_ge(s_vo, VM[("unpk", i)])
                if k >= 2:
                    pe.wait_ge(s_ao, AM[("wtevac", PLANES[k - 2])])
                else:
                    pe.wait_ge(s_ao, AM["wscol"])
                pt = (P2, P3)[k % 2]
                stg = wtrh[k % 2]
                last = None
                for b in range(KC):
                    last = pe.matmul(pt[:, b * 128:(b + 1) * 128],
                                     stg[:, b * 128:(b + 1) * 128],
                                     identb[:], start=True, stop=True)
                last.then_inc(s_tpw, 1)

            def ffn(tile, l3):
                tpr_base = T if l3 else 0
                s_ffn = s_ffn3 if l3 else s_ffn1
                gg = tpr_base + tile
                pe.wait_ge(s_tpr[gg % 2], 16 * (gg // 2 + 1))
                if tile >= 2:
                    pe.wait_ge(s_vo, VM[("epi3" if l3 else "epi1", tile - 2)])
                elif l3:
                    pe.wait_ge(s_vo, VM[("res", 14 + tile)])
                par = tile % 2
                last = None
                for hf in range(2):
                    for kc in range(KC):
                        pe.matmul(PF[par][:, hf * 512:(hf + 1) * 512],
                                  r1T[par][:, kc * 128:(kc + 1) * 128],
                                  AP(wt, 4 * hf * D + kc * 128,
                                     [[KC * D, 128], [D, 4], [1, 128]]),
                                  start=(kc == 0), stop=False)
                    last = pe.matmul(PF[par][:, hf * 512:(hf + 1) * 512],
                                     r1T[par][:, 1024:1152],
                                     bias2[:, hf * 512:(hf + 1) * 512],
                                     start=False, stop=True)
                last.then_inc(s_ffn, 1)

            if PHASE < 1:
                return
            pe.wait_ge(s_ao, AM[("wtevac", PLANES[7])])
            pe.wait_ge(s_lb, 32)
            pe.wait_ge(s_lds, 96)
            for t in range(T):
                ffn(t, False)

            if PHASE < 2:
                return

            def qkv_chunk(g2, i):
                proj, oc = divmod(i, 8)
                wsrc = (wqs, wks, wvs)[proj]
                if i == 0:
                    pe.wait_ge(s_tpy[0], 16 * (2 * g2 + 2))
                    pe.wait_ge(s_tpy[1], 16 * (2 * g2 + 2))
                if i >= 2:
                    pe.wait_ge(s_vo, VM[("evac", g2, i - 2)])
                elif g2 == 0:
                    pe.wait_ge(s_ao, AM[("wtevac", PLANES[7])])
                    if i == 0:
                        pe.wait_ge(s_lw, 16 * 24)
                else:
                    pe.wait_ge(s_vo, VM[("evac", g2 - 1, 22 + i)])
                ph = (i % 2) * 512
                last = None
                for kc in range(KC):
                    last = pe.matmul(P2[:, ph:ph + 512],
                                     wsrc[:, kc * D + oc * 128:kc * D + (oc + 1) * 128],
                                     AP(y2T[g2 % 2], kc * 128,
                                        [[4 * D, 128], [D, 4], [1, 128]]),
                                     start=(kc == 0), stop=(kc == KC - 1))
                last.then_inc(s_qkv, 1)

            def st_mm(tau):
                tl, par = tau % 4, tau % 2
                g2 = tau // 4
                pe.wait_ge(s_vo, VM[("evac", g2, 15)])
                if tau >= 2:
                    pe.wait_ge(s_vo, VM[("res", tau - 2)])
                else:
                    pe.wait_ge(s_vo, VM[("epi1", 14 + tau)])
                last = None
                for half in range(2):
                    for h in range(half, 16, 2):
                        hr = (h % 2) * 64
                        base = (h // 2) * 512 + tl * 128
                        for blk in range(2):
                            last = pe.matmul(
                                PF[par][blk * 64:(blk + 1) * 64, h * 64:(h + 1) * 64],
                                kf[hr:hr + 64, base + blk * 64:base + blk * 64 + 64],
                                qf[hr:hr + 64, base + blk * 64:base + blk * 64 + 64],
                                start=True, stop=True)
                    if half == 0:
                        last.then_inc(s_stb, 1)
                        pe.wait_ge(s_stb, tau + 1)
                last.then_inc(s_st, 1)

            def bc_mm(tau):
                par = tau % 2
                pe.wait_ge(s_ao, AM[("exp", tau)])
                if tau >= 1:
                    pe.wait_ge(s_ao, AM[("ln", tau - 1)])
                else:
                    pe.wait_ge(s_ao, AM[("wtevac", PLANES[7])])
                last = None
                for hf in range(2):
                    last = pe.matmul(P3[:, hf * 512:(hf + 1) * 512], blockones[:],
                                     eT[par][:, hf * 512:(hf + 1) * 512],
                                     start=True, stop=True)
                last.then_inc(s_bc, 1)

            def av_mm(tau):
                tl, par = tau % 4, tau % 2
                g2 = tau // 4
                pe.wait_ge(s_ao, AM[("exp", tau)])
                pe.wait_ge(s_vo, VM[("atn", tau)])
                pe.wait_ge(s_tpv[tau % 2], 16 * (tau // 2 + 1))
                last = None
                for h in range(16):
                    vcol = (h // 2) * 128 + (h % 2) * 64
                    for blk in range(2):
                        last = pe.matmul(
                            PF[par][blk * 64:(blk + 1) * 64, h * 64:(h + 1) * 64],
                            aTn[blk * 64:(blk + 1) * 64, h * 64:(h + 1) * 64],
                            vtp[par][blk * 64:(blk + 1) * 64, vcol:vcol + 64],
                            start=True, stop=True)
                last.then_inc(s_av, 1)

            for i in range(24):
                qkv_chunk(0, i)
            for g2 in range(G):
                t0 = g2 * 4
                st_mm(t0); st_mm(t0 + 1)
                bc_mm(t0); bc_mm(t0 + 1)
                av_mm(t0); av_mm(t0 + 1)
                st_mm(t0 + 2); st_mm(t0 + 3)
                bc_mm(t0 + 2); bc_mm(t0 + 3)
                av_mm(t0 + 2); av_mm(t0 + 3)
                if g2 + 1 < G:
                    for i in range(24):
                        qkv_chunk(g2 + 1, i)

            if PHASE < 3:
                return
            for t in range(T):
                ffn(t, True)

        @block.sync
        def _(sp):
            tpcnt = {}
            def tpose(semx, dst, srcx):
                c = tpcnt.get(id(semx), 0)
                if c:
                    sp.wait_ge(semx, c)
                sp.dma_start_transpose(dst, srcx).then_inc(semx, 16)
                tpcnt[id(semx)] = c + 16

            if PHASE < 1:
                return
            def y2tpose(t):
                g2 = t // 4
                sp.wait_ge(s_ao, AM[("y2", t)])
                if g2 >= 2:
                    sp.wait_ge(s_qkv, (g2 - 1) * 24)
                tpose(s_tpy[t % 2],
                      AP(y2T[g2 % 2], (t % 4) * D, [[KC * 512, 128], [128, 8], [1, 128]]),
                      AP(aT[t % 2], 0, [[D, 128], [1, D]]))

            for t in range(T):
                sp.wait_ge(s_vo, VM[("rnd1", t)])
                if t >= 2:
                    sp.wait_ge(s_ffn1, t - 1)
                tpose(s_tpr[t % 2],
                      r1T[t % 2][:].rearrange("p (c q) -> p c q", c=9),
                      r1x[t % 2][:].rearrange("p (c q) -> p c q", c=9))
                if PHASE >= 2 and 9 <= t <= 12:
                    y2tpose(t - 9)
            if PHASE < 2:
                return
            for g2 in range(G):
                if g2 >= 1:
                    for t in range(g2 * 4, g2 * 4 + 4):
                        y2tpose(t)
                if PHASE >= 3 and g2 == 1:
                    for t in range(2):
                        sp.wait_ge(s_vo, VM[("rnd3", t)])
                        tpose(s_tpr[t % 2],
                              r1T[t % 2][:].rearrange("p (c q) -> p c q", c=9),
                              r1x[t % 2][:].rearrange("p (c q) -> p c q", c=9))
                sp.wait_ge(s_vo, VM[("evac", g2, 23)])
                for tl in range(4):
                    tau = g2 * 4 + tl
                    if tau >= 2:
                        sp.wait_ge(s_av, tau - 1)
                    tpose(s_tpv[tau % 2],
                          AP(vtp[tau % 2], 0, [[D, 128], [128, 8], [1, 128]]),
                          AP(vf, tl * 1024, [[8 * 512, 128], [1, 1024]]))
            if PHASE < 3:
                return
            for t in range(2, T):
                sp.wait_ge(s_vo, VM[("rnd3", t)])
                sp.wait_ge(s_ffn3, t - 1)
                tpose(s_tpr[t % 2],
                      r1T[t % 2][:].rearrange("p (c q) -> p c q", c=9),
                      r1x[t % 2][:].rearrange("p (c q) -> p c q", c=9))

    mybir.codegen_inst_isa_subclasses(nc)
    return nc


_NC_CACHE = None


def kernel(L0_attn_x, L0_attn_Wq, L0_attn_bq, L0_attn_Wk, L0_attn_bk,
           L0_attn_Wv, L0_attn_bv, L0_ffn_packed_weight, L0_ffn_weight_scale,
           L0_ffn_bias):
    global _NC_CACHE
    _register_ops()
    if _NC_CACHE is None:
        _NC_CACHE = build_nc()
    nc = _NC_CACHE

    x = np.ascontiguousarray(np.asarray(L0_attn_x, dtype=np.float32).reshape(SEQ, D))
    com = {
        "wq": np.ascontiguousarray(np.asarray(L0_attn_Wq, np.float32)),
        "wk": np.ascontiguousarray(np.asarray(L0_attn_Wk, np.float32)),
        "wv": np.ascontiguousarray(np.asarray(L0_attn_Wv, np.float32)),
        "bq": np.ascontiguousarray(np.asarray(L0_attn_bq, np.float32)),
        "bk": np.ascontiguousarray(np.asarray(L0_attn_bk, np.float32)),
        "bv": np.ascontiguousarray(np.asarray(L0_attn_bv, np.float32)),
        "pkw": np.ascontiguousarray(np.asarray(L0_ffn_packed_weight, np.int32)),
        "wsc": np.ascontiguousarray(np.asarray(L0_ffn_weight_scale, np.float32).reshape(1)),
        "fbias": np.ascontiguousarray(np.asarray(L0_ffn_bias, np.float32)),
    }
    in_maps = [dict(com, x=np.ascontiguousarray(x[c * TOK:(c + 1) * TOK]))
               for c in range(NCORES)]
    res = run_bass_kernel_spmd(nc, in_maps, core_ids=list(range(NCORES)))
    out = np.concatenate([np.asarray(r["out"], np.float32) for r in res.results], axis=0)
    return out.reshape(1, SEQ, D)


# version 18
# speedup vs baseline: 1.1270x; 1.1270x over previous
import os
import numpy as np
import concourse.bass as bass
import concourse.mybir as mybir
from concourse.bass_utils import run_bass_kernel_spmd

F32 = mybir.dt.float32
BF16 = mybir.dt.bfloat16
I32 = mybir.dt.int32
AF = mybir.ActivationFunctionType
ALU = mybir.AluOpType
AP = bass.AP

NCORES = 8
D = 1024
SEQ = 16384
TOK = SEQ // NCORES
T = TOK // 128
G = T // 4
KC = D // 128
C_MAGIC = 12582912.0
EPS = 1e-5
PHASE = int(os.environ.get("ATH_PHASE", "3"))
PLANES = [0, 2, 4, 6, 1, 3, 5, 7]


def _register_ops():
    import concourse.dve_ops as dops
    if "ATH_RND" in dops.CUSTOM_DVE_REFS:
        globals()["_OP_RND"] = next(o for o in dops.OPS if o.name == "ATH_RND")
        globals()["_OP_AFF"] = next(o for o in dops.OPS if o.name == "ATH_AFFMAX")
        return
    from concourse.dve_op import DveOp
    from concourse.dve_spec import Spec, Src0, C0, C1, C2, Zero, maxx, lower
    from concourse.dve_uop import AluOp as UAluOp

    spec_rnd = Spec(body=((Src0 * C0 + C1) + C2) - C2)
    sha_rnd = {v: lower(spec_rnd, ver=v, perf_en=False, name="ATH_RND").sha(v)
               for v in ("v3", "v4")}
    op_rnd = DveOp("ATH_RND", spec_rnd, subdim=False, uops_sha=sha_rnd,
                   reference=lambda in0, in1, s0, s1, imm2: np.rint(in0 * s0 + s1))

    _aff = Src0 * C0 + C1
    spec_aff = Spec(body=maxx(_aff, Zero - _aff), accum=UAluOp.MAX, accum_init=Zero)
    sha_aff = {v: lower(spec_aff, ver=v, perf_en=False, name="ATH_AFFMAX").sha(v)
               for v in ("v3", "v4")}
    def _ref_aff(in0, in1, s0, s1, imm2):
        r = np.abs(in0 * s0 + s1)
        return r, np.max(r, axis=-1, keepdims=True)
    op_aff = DveOp("ATH_AFFMAX", spec_aff, subdim=False, uops_sha=sha_aff,
                   reference=_ref_aff)

    for op in (op_rnd, op_aff):
        dops.OPS.append(op)
        dops.CUSTOM_DVE_SPECS[op.name] = op.spec
        dops.CUSTOM_DVE_REFS[op.name] = op.ref
        dops._SUB_OPCODE_FOR_NAME[op.name] = dops._CUSTOM_DVE_ROW_BASE + len(dops.OPS) - 1
    globals()["_OP_RND"] = op_rnd
    globals()["_OP_AFF"] = op_aff


class _Ser:
    def __init__(self, eng, sem, mstones, counting):
        self.eng, self.sem, self.m, self.counting = eng, sem, mstones, counting
        self.n = 0

    def op(self, name, *args, key=None, **kw):
        if not self.counting:
            if self.n:
                self.eng.wait_ge(self.sem, self.n)
            getattr(self.eng, name)(*args, **kw).then_inc(self.sem, 1)
        self.n += 1
        if key is not None:
            self.m[key] = self.n

    def wait(self, sem, val):
        if not self.counting:
            self.eng.wait_ge(sem, val)


def build_nc():
    _register_ops()
    op_rnd, op_aff = globals()["_OP_RND"], globals()["_OP_AFF"]

    nc = bass.Bass("TRN2", target_bir_lowering=False)

    x_in = nc.dram_tensor("x", [TOK, D], F32, kind="ExternalInput").ap()
    wq_d = nc.dram_tensor("wq", [D, D], F32, kind="ExternalInput").ap()
    wk_d = nc.dram_tensor("wk", [D, D], F32, kind="ExternalInput").ap()
    wv_d = nc.dram_tensor("wv", [D, D], F32, kind="ExternalInput").ap()
    bq_d = nc.dram_tensor("bq", [D], F32, kind="ExternalInput").ap()
    bk_d = nc.dram_tensor("bk", [D], F32, kind="ExternalInput").ap()
    bv_d = nc.dram_tensor("bv", [D], F32, kind="ExternalInput").ap()
    pk_d = nc.dram_tensor("pkw", [D * D // 4], I32, kind="ExternalInput").ap()
    ws_d = nc.dram_tensor("wsc", [1], F32, kind="ExternalInput").ap()
    bb_d = nc.dram_tensor("fbias", [D], F32, kind="ExternalInput").ap()
    out_d = nc.dram_tensor("out", [TOK, D], F32, kind="ExternalOutput").ap()
    blo_scr = nc.dram_tensor("bloscr", [D], BF16).ap()

    from contextlib import ExitStack
    es = ExitStack()
    with es:
        def sb(name, shape, dt):
            return es.enter_context(nc.sbuf_tensor(name, shape, dt))
        def sem(name):
            return es.enter_context(nc.semaphore(name))

        xs = sb("xs", [128, T * D], F32)
        wqs = sb("wqs", [128, KC * D], BF16)
        wks = sb("wks", [128, KC * D], BF16)
        wvs = sb("wvs", [128, KC * D], BF16)
        wt = sb("wt", [128, KC * D], BF16)
        y2T = [sb(f"y2T{i}", [128, KC * 512], BF16) for i in range(2)]
        qf = sb("qf", [128, 8 * 512], BF16)
        kf = sb("kf", [128, 8 * 512], BF16)
        vf = sb("vf", [128, 8 * 512], BF16)
        vtp = [sb(f"vtp{i}", [128, D], BF16) for i in range(2)]
        rbf = sb("rbf", [128, D], F32)
        r1x = [sb(f"r1x{i}", [128, 1152], BF16) for i in range(2)]
        r1T = [sb(f"r1T{i}", [128, 1152], BF16) for i in range(2)]
        aT = [sb(f"aT{i}", [128, D], BF16) for i in range(2)]
        eT = [sb(f"eT{i}", [128, D], BF16) for i in range(2)]
        aTn = sb("aTn", [128, D], BF16)
        rb = sb("rb", [128, D], BF16)
        bias2 = sb("bias2", [128, D], BF16)
        pkr = sb("pkr", [128, D], I32)
        pkh = sb("pkh", [128, D // 2], I32)
        wtrh = [rb, aTn]
        bcols = sb("bcols", [128, 24], F32)
        bq8 = sb("bq8", [128, 8], F32)
        blockones = sb("blockones", [128, 128], BF16)
        identb = sb("identb", [128, 128], BF16)
        onesrow = sb("onesrow", [1, 128], F32)
        wssb = sb("wssb", [1, 1], F32)
        brows = sb("brows", [24, 128], F32)
        identf = sb("identf", [24, 24], F32)
        sums2 = sb("sums2", [128, 2 * T], F32)
        sums1 = sb("sums1", [128, 4], F32)
        bnst = sb("bnst", [128, 12], F32)
        mvh = sb("mvh", [128, 4], F32)
        var_c = sb("var_c", [128, 2], F32)
        t1_c = sb("t1_c", [128, 1], F32)
        t2_c = sb("t2_c", [128, 1], F32)
        ta_c = sb("ta_c", [128, 1], F32)
        tb_c = sb("tb_c", [128, 1], F32)
        rstd_c = sb("rstd_c", [128, 4], F32)
        mneg_c = sb("mneg_c", [128, 2], F32)
        bln_c = sb("bln_c", [128, 2], F32)
        bA_c = sb("bA_c", [128, 1], F32)
        amax_c = sb("amax_c", [128, 1], F32)
        g_col = sb("g_col", [128, 1], F32)
        h_col = sb("h_col", [128, 1], F32)
        c_all = sb("c_all", [128, T], F32)
        wscol = sb("wscol", [128, 1], F32)
        wsr = sb("wsr", [128, 1], F32)
        wsc2 = sb("wsc2", [128, 1], F32)
        eps_c = sb("eps_c", [128, 1], F32)
        tiny_c = sb("tiny_c", [128, 1], F32)

        P0 = es.enter_context(nc.psum_tensor("P0", [128, D], F32))
        P1 = es.enter_context(nc.psum_tensor("P1", [128, D], F32))
        P2 = es.enter_context(nc.psum_tensor("P2", [128, D], F32))
        P3 = es.enter_context(nc.psum_tensor("P3", [128, D], F32))
        PF = [P0, P1]

        s_g = sem("s_g"); s_out = sem("s_out")
        s_lx4 = [sem(f"s_lx{j}") for j in range(4)]
        s_lds = sem("s_lds"); s_lpk = sem("s_lpk")
        s_lws = sem("s_lws")
        s_lb = sem("s_lb"); s_lw = sem("s_lw")
        s_vo = sem("s_vo"); s_ao = sem("s_ao")
        s_peinit = sem("s_peinit")
        s_tpw = sem("s_tpw")
        s_tpr = [sem(f"s_tpr{i}") for i in range(2)]
        s_tpy = [sem(f"s_tpy{i}") for i in range(2)]
        s_tpv = [sem(f"s_tpv{i}") for i in range(2)]
        s_ffn1 = sem("s_ffn1"); s_qkv = sem("s_qkv")
        s_st = sem("s_st"); s_stb = sem("s_stb"); s_bc = sem("s_bc"); s_av = sem("s_av")
        s_ffn3 = sem("s_ffn3")

        VM, AM = {}, {}

        def dve_body(S):
            S.op("tensor_scalar", identb[:], identb[:], 0.0, None, ALU.is_equal,
                 key="identb")
            S.op("tensor_scalar", identf[:], identf[:], 0.0, None, ALU.is_equal,
                 key="identf")
            for k, i in enumerate(PLANES):
                S.wait(s_lpk, 16 if k < 4 else 32)
                if k >= 2:
                    S.wait(s_tpw, k - 1)
                for h in range(2):
                    pkey = "pk0done" if (k == 3 and h == 1) else None
                    seg = slice(h * 512, (h + 1) * 512)
                    if i // 2 == 0:
                        S.op("tensor_scalar", pkh[:], pkr[:, seg], 3, None,
                             ALU.bitwise_and, key=pkey)
                    else:
                        S.op("tensor_scalar", pkh[:], pkr[:, seg], 2 * (i // 2), 3,
                             ALU.logical_shift_right, ALU.bitwise_and, key=pkey)
                    S.op("tensor_scalar", wtrh[k % 2][:, seg], pkh[:], 1.0, None,
                         ALU.subtract, key=("unpk", i) if h == 1 else None)
                if k == 0:
                    S.wait(s_lds, 96)
                    S.op("tensor_tensor", aT[0][0:1, :], rbf[0:1, :],
                         bias2[0:1, :], ALU.subtract, key="blo")
            S.wait(s_ao, AM.get("wscol", 0))
            S.op("reciprocal", wsr[:], wscol[:])
            S.op("tensor_scalar", wsc2[:], wscol[:], 0.5 / 127.0, None, ALU.mult)

            def stats(tile, l3):
                par = tile % 2
                xt = xs[:, tile * D:(tile + 1) * D]
                if not l3:
                    S.wait(s_lx4[tile // 4], 16)
                if tile >= 2:
                    S.wait(s_ao, AM.get(("rstd3" if l3 else "rstd1", tile - 2), 0))
                S.op("bn_stats", bnst[:, 0:6], xt[:, 0:512])
                S.op("bn_stats", bnst[:, 6:12], xt[:, 512:1024])
                S.op("bn_aggr", mvh[:, 0:2], bnst[:, 0:6])
                S.op("bn_aggr", mvh[:, 2:4], bnst[:, 6:12])
                S.op("tensor_tensor", t1_c[:], mvh[:, 0:1], mvh[:, 2:3], ALU.add)
                S.op("tensor_scalar", mneg_c[:, par:par + 1], t1_c[:], -0.5,
                     None, ALU.mult)
                S.op("tensor_tensor", t1_c[:], mvh[:, 0:1], mvh[:, 2:3], ALU.subtract)
                S.op("tensor_tensor", t1_c[:], t1_c[:], t1_c[:], ALU.mult)
                S.op("tensor_tensor", t2_c[:], mvh[:, 1:2], mvh[:, 3:4], ALU.add)
                S.op("tensor_scalar", t2_c[:], t2_c[:], 0.5, None, ALU.mult)
                S.op("scalar_tensor_tensor", var_c[:, par:par + 1], t1_c[:], 0.25,
                     t2_c[:], ALU.mult, ALU.add, key=("stat3" if l3 else "stat1", tile))

            def quant_ffn(tile, l3):
                par = tile % 2
                g = (T if l3 else 0) + tile
                if g >= 2:
                    S.wait(s_tpr[g % 2], 16 * (g // 2))
                S.wait(s_ao, AM.get(("rstd3" if l3 else "rstd1", tile), 0))
                rsc = rstd_c[:, par:par + 1]
                blc = bln_c[:, par:par + 1]
                S.op("_custom_dve", op_aff, out=aTn[:],
                     in0=xs[:, tile * D:(tile + 1) * D],
                     s0=rsc, s1=blc, accum_out=amax_c[:])
                S.op("tensor_scalar", amax_c[:], amax_c[:], 1e-5, None, ALU.max)
                S.op("reciprocal", t1_c[:], amax_c[:])
                S.op("tensor_scalar", g_col[:], t1_c[:], 127.0, rsc,
                     ALU.mult, ALU.mult)
                S.op("tensor_scalar", h_col[:], t1_c[:], 127.0, blc,
                     ALU.mult, ALU.mult, key=("qh", l3, tile))
                S.op("tensor_tensor", c_all[:, tile:tile + 1], amax_c[:], wsc2[:], ALU.mult)
                S.op("tensor_scalar", t2_c[:], t1_c[:], 127.0, wsr[:],
                     ALU.mult, ALU.mult)
                S.op("tensor_copy", r1x[par][:, 1024:1025], t2_c[:])
                S.op("tensor_tensor", r1x[par][:, 1025:1026], t2_c[:],
                     r1x[par][:, 1024:1025], ALU.subtract)
                S.op("tensor_copy", r1x[par][:, 1026:1027], r1x[par][:, 1024:1025])
                S.op("_custom_dve", op_rnd, out=r1x[par][:, 0:1024],
                     in0=xs[:, tile * D:(tile + 1) * D],
                     s0=g_col[:], s1=h_col[:], imm2=C_MAGIC,
                     key=("rnd3" if l3 else "rnd1", tile))

            def epi_ffn(tile, l3):
                S.wait(s_ffn3 if l3 else s_ffn1, tile + 1)
                S.op("scalar_tensor_tensor",
                     xs[:, tile * D:(tile + 1) * D], PF[tile % 2][:],
                     c_all[:, tile:tile + 1], xs[:, tile * D:(tile + 1) * D],
                     ALU.mult, ALU.add, key=("epi3" if l3 else "epi1", tile))

            def ffn_layer(l3, t0=0):
                for t in range(T):
                    if t >= t0:
                        quant_ffn(t, l3)
                    if t >= 1:
                        epi_ffn(t - 1, l3)
                epi_ffn(T - 1, l3)

            def evacs(g2):
                for i in range(24):
                    proj, oc = divmod(i, 8)
                    S.wait(s_qkv, g2 * 24 + i + 1)
                    if i == 0 and g2 >= 1:
                        S.wait(s_st, 4 * g2)
                        S.wait(s_tpv[0], 16 * g2 * 2)
                        S.wait(s_tpv[1], 16 * g2 * 2)
                    ph = (i % 2) * 512
                    if proj == 0:
                        S.op("tensor_scalar", qf[:, oc * 512:(oc + 1) * 512],
                             P2[:, ph:ph + 512], 0.125, bq8[:, oc:oc + 1],
                             ALU.mult, ALU.add, key=("evac", g2, i))
                    elif proj == 1:
                        S.op("tensor_scalar", kf[:, oc * 512:(oc + 1) * 512],
                             P2[:, ph:ph + 512], bcols[:, 8 + oc:9 + oc], None,
                             ALU.add, key=("evac", g2, i))
                    else:
                        for tl in range(4):
                            S.op("tensor_scalar",
                                 vf[:, tl * 1024 + oc * 128:tl * 1024 + (oc + 1) * 128],
                                 P2[:, ph + tl * 128:ph + (tl + 1) * 128],
                                 bcols[:, 16 + oc:17 + oc], None, ALU.add,
                                 key=("evac", g2, i) if tl == 3 else None)

            if PHASE < 1:
                return
            S.wait(s_tpw, 8)
            ffn_layer(False)
            if PHASE < 2:
                return
            evacs(0)
            for g2 in range(G):
                for tau in range(g2 * 4, g2 * 4 + 4):
                    S.wait(s_ao, AM.get(("rcp", tau), 0))
                    S.op("tensor_tensor", aTn[:], eT[tau % 2][:], rb[:],
                         ALU.mult, key=("atn", tau))
                    S.wait(s_av, tau + 1)
                    S.op("tensor_tensor", xs[:, tau * D:(tau + 1) * D],
                         xs[:, tau * D:(tau + 1) * D], PF[tau % 2][:], ALU.add,
                         key=("res", tau))
                    if PHASE >= 3 and g2 == 0:
                        if tau == 1:
                            stats(0, True)
                        elif tau == 2:
                            stats(1, True)
                        elif tau == 3:
                            quant_ffn(0, True)
                if g2 + 1 < G:
                    evacs(g2 + 1)
                if PHASE >= 3 and g2 == 0:
                    quant_ffn(1, True)
            if PHASE < 3:
                return
            ffn_layer(True, t0=2)

        def act_body(S):
            S.wait(s_peinit, 2)
            S.op("copy", bcols[:], P2[:, 0:24])
            S.op("activation", bq8[:], bcols[:, 0:8], AF.Identity, scale=0.125)
            S.op("copy", wscol[:], P3[:, 0:1], key="wscol")
            for k, i in enumerate(PLANES):
                S.wait(s_tpw, k + 1)
                S.op("copy", wt[:, i * D:(i + 1) * D], (P2, P3)[k % 2][:],
                     key=("wtevac", i))

            def statblk(t, l3):
                par = t % 2
                sx = sums1[:, 2 * par:2 * par + 1]
                sq = sums1[:, 2 * par + 1:2 * par + 2]
                xt = xs[:, t * D:(t + 1) * D]
                if l3:
                    S.wait(s_vo, VM[("res", t)])
                    if t == 2:
                        S.wait(s_vo, VM[("atn", T - 1)])
                else:
                    S.wait(s_lx4[t // 4], 16)
                if t >= 2:
                    S.wait(s_vo, VM.get(("qh", l3, t - 2), 0))
                S.op("activation", eT[0][:], xt, AF.Identity, accum_out=sx)
                S.op("activation", eT[1][:], xt, AF.Square, accum_out=sq)
                S.op("activation", mneg_c[:, par:par + 1], sx, AF.Identity,
                     scale=-1.0 / D)
                S.op("activation", ta_c[:], mneg_c[:, par:par + 1], AF.Square)
                S.op("activation", tb_c[:], ta_c[:], AF.Identity, scale=-1.0,
                     bias=eps_c[:])
                S.op("activation", ta_c[:], sq, AF.Ln, scale=1.0 / D, bias=tb_c[:])
                S.op("activation", rstd_c[:, par:par + 1], ta_c[:], AF.Exp,
                     scale=-0.5)
                S.op("activation", bln_c[:, par:par + 1], mneg_c[:, par:par + 1],
                     AF.Identity, scale=rstd_c[:, par:par + 1],
                     key=("rstd3" if l3 else "rstd1", t))

            def rstdblk(t, l3):
                par = t % 2
                S.wait(s_vo, VM.get(("stat3" if l3 else "stat1", t), 0))
                if t >= 2:
                    S.wait(s_vo, VM.get(("qh", l3, t - 2), 0))
                S.op("activation", ta_c[:], var_c[:, par:par + 1], AF.Ln,
                     bias=eps_c[:], scale=1.0)
                S.op("activation", rstd_c[:, par:par + 1], ta_c[:], AF.Exp,
                     scale=-0.5)
                S.op("activation", bln_c[:, par:par + 1], mneg_c[:, par:par + 1],
                     AF.Identity, scale=rstd_c[:, par:par + 1],
                     key=("rstd3" if l3 else "rstd1", t))

            def accum2(j):
                xt = xs[:, j * D:(j + 1) * D]
                S.wait(s_vo, VM[("epi1", j)])
                S.op("activation", eT[0][:], xt, AF.Identity,
                     accum_out=sums2[:, 2 * j:2 * j + 1])
                S.op("activation", eT[1][:], xt, AF.Square,
                     accum_out=sums2[:, 2 * j + 1:2 * j + 2], key=("st2", j))

            def y2tile(t):
                    c = 2 + t % 2
                    S.op("activation", tb_c[:], sums2[:, 2 * t:2 * t + 1],
                         AF.Identity, scale=-1.0 / D)
                    S.op("activation", bA_c[:], tb_c[:], AF.Square)
                    S.op("activation", ta_c[:], bA_c[:], AF.Identity, scale=-1.0,
                         bias=eps_c[:])
                    S.op("activation", bA_c[:], sums2[:, 2 * t + 1:2 * t + 2],
                         AF.Ln, scale=1.0 / D, bias=ta_c[:])
                    S.op("activation", rstd_c[:, c:c + 1], bA_c[:], AF.Exp,
                         scale=-0.5, key=("rstd2", t))
                    S.op("activation", bA_c[:], tb_c[:], AF.Identity,
                         scale=rstd_c[:, c:c + 1])
                    if t == 0:
                        S.wait(s_vo, VM["blo"])
                        S.wait(s_lb, 16)
                    if t >= 2:
                        S.wait(s_tpy[t % 2], 16 * (t // 2))
                    S.op("activation", aT[t % 2][:], xs[:, t * D:(t + 1) * D],
                         AF.Identity, bias=bA_c[:], scale=rstd_c[:, c:c + 1],
                         key=("y2", t))

            def y2blk(g2, lo=0, hi=4):
                for t in range(g2 * 4 + lo, g2 * 4 + hi):
                    y2tile(t)

            def exps(g2, lo, hi):
                for tau in range(g2 * 4 + lo, g2 * 4 + hi):
                    S.wait(s_st, tau + 1)
                    if tau >= 2:
                        S.wait(s_bc, tau - 1)
                        S.wait(s_vo, VM.get(("atn", tau - 2), 0))
                    S.op("activation", eT[tau % 2][:], PF[tau % 2][:], AF.Exp,
                         key=("exp", tau))

            def rcps(g2, lo, hi):
                for tau in range(g2 * 4 + lo, g2 * 4 + hi):
                    S.wait(s_bc, tau + 1)
                    S.op("activation", rbf[:], P3[:], AF.Ln, bias=tiny_c[:],
                         scale=1.0, key=("ln", tau))
                    if tau >= 1:
                        S.wait(s_vo, VM.get(("atn", tau - 1), 0))
                    S.op("activation", rb[:], rbf[:], AF.Exp, scale=-1.0,
                         key=("rcp", tau))

            if PHASE < 1:
                return
            for t in range(T):
                statblk(t, False)
                if PHASE >= 2 and t >= 4:
                    accum2(t - 4)
                if PHASE >= 2 and 8 <= t <= 11:
                    y2tile(t - 8)
            if PHASE < 2:
                return
            for j in range(12, 16):
                accum2(j)
            for g2 in range(G):
                exps(g2, 0, 2)
                rcps(g2, 0, 2)
                if g2 + 1 < G:
                    y2blk(g2 + 1, 0, 2)
                exps(g2, 2, 4)
                if PHASE >= 3 and g2 == 0:
                    rstdblk(0, True)
                if g2 + 1 < G:
                    y2blk(g2 + 1, 2, 4)
                rcps(g2, 2, 4)
                if PHASE >= 3 and g2 == 0:
                    rstdblk(1, True)
            if PHASE < 3:
                return
            for t in range(2, T):
                statblk(t, True)

        for _ in range(2):
            dve_body(_Ser(None, None, VM, True))
            act_body(_Ser(None, None, AM, True))

        block = es.enter_context(nc.Block())

        @block.gpsimd
        def _(g):
            g.iota(identb[:], [[1, 128]], channel_multiplier=-1,
                   allow_small_or_imprecise_dtypes=True).then_inc(s_g, 1)
            g.iota(identf[:], [[1, 24]], channel_multiplier=-1,
                   allow_small_or_imprecise_dtypes=True).then_inc(s_g, 1)
            g.memzero(bias2[:]).then_inc(s_g, 1)
            g.memset(r1x[0][:, 1027:1152], 0).then_inc(s_g, 1)
            g.memset(r1x[1][:, 1027:1152], 0).then_inc(s_g, 1)
            g.memset(blockones[0:64, 0:64], 1.0).then_inc(s_g, 1)
            g.memset(blockones[0:64, 64:128], 0).then_inc(s_g, 1)
            g.memset(blockones[64:128, 0:64], 0).then_inc(s_g, 1)
            g.memset(blockones[64:128, 64:128], 1.0).then_inc(s_g, 1)
            g.memset(onesrow[:], 1.0).then_inc(s_g, 1)
            g.memset(eps_c[:], EPS).then_inc(s_g, 1)
            g.memset(tiny_c[:], 1e-35).then_inc(s_g, 1)
            g.wait_ge(s_g, 12)
            g.dma_start(pkr[:], pk_d.rearrange("(gg p n) -> gg p n", gg=2, p=128)[0]).then_inc(s_lpk, 16)
            g.dma_start(wssb[:], ws_d.rearrange("(a o) -> a o", a=1)).then_inc(s_lws, 16)
            g.dma_start(bias2[0:1, :], bb_d.rearrange("(a o) -> a o", a=1)).then_inc(s_lds, 16)
            g.dma_start(bias2[1:2, :], bb_d.rearrange("(a o) -> a o", a=1)).then_inc(s_lds, 16)
            g.dma_start(brows[0:8, :], bq_d.rearrange("(c p) -> c p", p=128)).then_inc(s_lds, 16)
            g.dma_start(brows[8:16, :], bk_d.rearrange("(c p) -> c p", p=128)).then_inc(s_lds, 16)
            g.dma_start(brows[16:24, :], bv_d.rearrange("(c p) -> c p", p=128)).then_inc(s_lds, 16)
            g.dma_start(rbf[0:1, :], bb_d.rearrange("(a o) -> a o", a=1)).then_inc(s_lds, 16)
            g.wait_ge(s_vo, VM["pk0done"])
            g.dma_start(pkr[:], pk_d.rearrange("(gg p n) -> gg p n", gg=2, p=128)[1]).then_inc(s_lpk, 16)
            g.wait_ge(s_vo, VM["blo"])
            g.dma_start(blo_scr.rearrange("(a o) -> a o", a=1), aT[0][0:1, :]).then_inc(s_lb, 16)
            g.wait_ge(s_lb, 16)
            g.dma_start(bias2[2:3, :], blo_scr.rearrange("(a o) -> a o", a=1)).then_inc(s_lb, 16)
            for wdst, wsrc in ((wqs, wq_d), (wks, wk_d), (wvs, wv_d)):
                for c in range(KC):
                    g.dma_start(wdst[:, c * D:(c + 1) * D],
                                wsrc[c * 128:(c + 1) * 128, :]).then_inc(s_lw, 16)
            for t in range(T):
                if PHASE >= 3:
                    g.wait_ge(s_vo, VM[("epi3", t)])
                elif PHASE == 2:
                    g.wait_ge(s_vo, VM[("res", t)])
                elif PHASE == 1:
                    g.wait_ge(s_vo, VM[("epi1", t)])
                else:
                    g.wait_ge(s_lx4[t // 4], 16)
                g.dma_start(out_d[t * 128:(t + 1) * 128, :],
                            xs[:, t * D:(t + 1) * D]).then_inc(s_out, 16)
            g.wait_ge(s_out, 16 * T)

        @block.scalar
        def _(a):
            for j in range(4):
                a.dma_start(xs[:, j * 4 * D:(j + 1) * 4 * D].rearrange("p (t d) -> p t d", t=4),
                            x_in[j * 512:(j + 1) * 512].rearrange("(t p) d -> p t d", p=128)
                            ).then_inc(s_lx4[j], 16)
            act_body(_Ser(a, s_ao, AM, False))

        @block.vector
        def _(v):
            v.wait_ge(s_g, 12)
            dve_body(_Ser(v, s_vo, VM, False))

        @block.tensor
        def _(pe):
            pe.wait_ge(s_g, 12)
            pe.wait_ge(s_vo, VM["identf"])
            pe.wait_ge(s_lws, 16)
            pe.wait_ge(s_lds, 96)
            pe.transpose(P2[:, 0:24], brows[0:24, :], identf[0:24, 0:24]).then_inc(s_peinit, 1)
            pe.matmul(P3[:, 0:1], onesrow[0:1, :], wssb[0:1, 0:1],
                      start=True, stop=True).then_inc(s_peinit, 1)
            for k, i in enumerate(PLANES):
                pe.wait_ge(s_vo, VM[("unpk", i)])
                if k >= 2:
                    pe.wait_ge(s_ao, AM[("wtevac", PLANES[k - 2])])
                else:
                    pe.wait_ge(s_ao, AM["wscol"])
                pt = (P2, P3)[k % 2]
                stg = wtrh[k % 2]
                last = None
                for b in range(KC):
                    last = pe.matmul(pt[:, b * 128:(b + 1) * 128],
                                     stg[:, b * 128:(b + 1) * 128],
                                     identb[:], start=True, stop=True)
                last.then_inc(s_tpw, 1)

            def ffn(tile, l3):
                tpr_base = T if l3 else 0
                s_ffn = s_ffn3 if l3 else s_ffn1
                gg = tpr_base + tile
                pe.wait_ge(s_tpr[gg % 2], 16 * (gg // 2 + 1))
                if tile >= 2:
                    pe.wait_ge(s_vo, VM[("epi3" if l3 else "epi1", tile - 2)])
                elif l3:
                    pe.wait_ge(s_vo, VM[("res", 14 + tile)])
                par = tile % 2
                last = None
                for hf in range(2):
                    for kc in range(KC):
                        pe.matmul(PF[par][:, hf * 512:(hf + 1) * 512],
                                  r1T[par][:, kc * 128:(kc + 1) * 128],
                                  AP(wt, 4 * hf * D + kc * 128,
                                     [[KC * D, 128], [D, 4], [1, 128]]),
                                  start=(kc == 0), stop=False)
                    last = pe.matmul(PF[par][:, hf * 512:(hf + 1) * 512],
                                     r1T[par][:, 1024:1152],
                                     bias2[:, hf * 512:(hf + 1) * 512],
                                     start=False, stop=True)
                last.then_inc(s_ffn, 1)

            if PHASE < 1:
                return
            pe.wait_ge(s_ao, AM[("wtevac", PLANES[7])])
            pe.wait_ge(s_lb, 32)
            pe.wait_ge(s_lds, 96)
            for t in range(T):
                ffn(t, False)

            if PHASE < 2:
                return

            def qkv_chunk(g2, i):
                proj, oc = divmod(i, 8)
                wsrc = (wqs, wks, wvs)[proj]
                if i == 0:
                    pe.wait_ge(s_tpy[0], 16 * (2 * g2 + 2))
                    pe.wait_ge(s_tpy[1], 16 * (2 * g2 + 2))
                if i >= 2:
                    pe.wait_ge(s_vo, VM[("evac", g2, i - 2)])
                elif g2 == 0:
                    pe.wait_ge(s_ao, AM[("wtevac", PLANES[7])])
                    if i == 0:
                        pe.wait_ge(s_lw, 16 * 24)
                else:
                    pe.wait_ge(s_vo, VM[("evac", g2 - 1, 22 + i)])
                ph = (i % 2) * 512
                last = None
                for kc in range(KC):
                    last = pe.matmul(P2[:, ph:ph + 512],
                                     wsrc[:, kc * D + oc * 128:kc * D + (oc + 1) * 128],
                                     AP(y2T[g2 % 2], kc * 128,
                                        [[4 * D, 128], [D, 4], [1, 128]]),
                                     start=(kc == 0), stop=(kc == KC - 1))
                last.then_inc(s_qkv, 1)

            def st_mm(tau):
                tl, par = tau % 4, tau % 2
                g2 = tau // 4
                pe.wait_ge(s_vo, VM[("evac", g2, 15)])
                if tau >= 2:
                    pe.wait_ge(s_vo, VM[("res", tau - 2)])
                else:
                    pe.wait_ge(s_vo, VM[("epi1", 14 + tau)])
                last = None
                for half in range(2):
                    for h in range(half, 16, 2):
                        hr = (h % 2) * 64
                        base = (h // 2) * 512 + tl * 128
                        for blk in range(2):
                            last = pe.matmul(
                                PF[par][blk * 64:(blk + 1) * 64, h * 64:(h + 1) * 64],
                                kf[hr:hr + 64, base + blk * 64:base + blk * 64 + 64],
                                qf[hr:hr + 64, base + blk * 64:base + blk * 64 + 64],
                                start=True, stop=True)
                    if half == 0:
                        last.then_inc(s_stb, 1)
                        pe.wait_ge(s_stb, tau + 1)
                last.then_inc(s_st, 1)

            def bc_mm(tau):
                par = tau % 2
                pe.wait_ge(s_ao, AM[("exp", tau)])
                if tau >= 1:
                    pe.wait_ge(s_ao, AM[("ln", tau - 1)])
                else:
                    pe.wait_ge(s_ao, AM[("wtevac", PLANES[7])])
                last = None
                for hf in range(2):
                    last = pe.matmul(P3[:, hf * 512:(hf + 1) * 512], blockones[:],
                                     eT[par][:, hf * 512:(hf + 1) * 512],
                                     start=True, stop=True)
                last.then_inc(s_bc, 1)

            def av_mm(tau):
                tl, par = tau % 4, tau % 2
                g2 = tau // 4
                pe.wait_ge(s_ao, AM[("exp", tau)])
                pe.wait_ge(s_vo, VM[("atn", tau)])
                pe.wait_ge(s_tpv[tau % 2], 16 * (tau // 2 + 1))
                last = None
                for h in range(16):
                    vcol = (h // 2) * 128 + (h % 2) * 64
                    for blk in range(2):
                        last = pe.matmul(
                            PF[par][blk * 64:(blk + 1) * 64, h * 64:(h + 1) * 64],
                            aTn[blk * 64:(blk + 1) * 64, h * 64:(h + 1) * 64],
                            vtp[par][blk * 64:(blk + 1) * 64, vcol:vcol + 64],
                            start=True, stop=True)
                last.then_inc(s_av, 1)

            for i in range(24):
                qkv_chunk(0, i)
            for g2 in range(G):
                t0 = g2 * 4
                st_mm(t0); st_mm(t0 + 1)
                bc_mm(t0); bc_mm(t0 + 1)
                av_mm(t0); av_mm(t0 + 1)
                st_mm(t0 + 2); st_mm(t0 + 3)
                bc_mm(t0 + 2); bc_mm(t0 + 3)
                av_mm(t0 + 2); av_mm(t0 + 3)
                if g2 + 1 < G:
                    for i in range(24):
                        qkv_chunk(g2 + 1, i)

            if PHASE < 3:
                return
            for t in range(T):
                ffn(t, True)

        @block.sync
        def _(sp):
            tpcnt = {}
            def tpose(semx, dst, srcx):
                c = tpcnt.get(id(semx), 0)
                if c:
                    sp.wait_ge(semx, c)
                sp.dma_start_transpose(dst, srcx).then_inc(semx, 16)
                tpcnt[id(semx)] = c + 16

            if PHASE < 1:
                return
            def y2tpose(t):
                g2 = t // 4
                sp.wait_ge(s_ao, AM[("y2", t)])
                if g2 >= 2:
                    sp.wait_ge(s_qkv, (g2 - 1) * 24)
                tpose(s_tpy[t % 2],
                      AP(y2T[g2 % 2], (t % 4) * D, [[KC * 512, 128], [128, 8], [1, 128]]),
                      AP(aT[t % 2], 0, [[D, 128], [1, D]]))

            for t in range(T):
                sp.wait_ge(s_vo, VM[("rnd1", t)])
                if t >= 2:
                    sp.wait_ge(s_ffn1, t - 1)
                tpose(s_tpr[t % 2],
                      r1T[t % 2][:].rearrange("p (c q) -> p c q", c=9),
                      r1x[t % 2][:].rearrange("p (c q) -> p c q", c=9))
                if PHASE >= 2 and 9 <= t <= 12:
                    y2tpose(t - 9)
            if PHASE < 2:
                return
            for g2 in range(G):
                if g2 >= 1:
                    for t in range(g2 * 4, g2 * 4 + 4):
                        y2tpose(t)
                if PHASE >= 3 and g2 == 1:
                    for t in range(2):
                        sp.wait_ge(s_vo, VM[("rnd3", t)])
                        tpose(s_tpr[t % 2],
                              r1T[t % 2][:].rearrange("p (c q) -> p c q", c=9),
                              r1x[t % 2][:].rearrange("p (c q) -> p c q", c=9))
                sp.wait_ge(s_vo, VM[("evac", g2, 23)])
                for tl in range(4):
                    tau = g2 * 4 + tl
                    if tau >= 2:
                        sp.wait_ge(s_av, tau - 1)
                    tpose(s_tpv[tau % 2],
                          AP(vtp[tau % 2], 0, [[D, 128], [128, 8], [1, 128]]),
                          AP(vf, tl * 1024, [[8 * 512, 128], [1, 1024]]))
            if PHASE < 3:
                return
            for t in range(2, T):
                sp.wait_ge(s_vo, VM[("rnd3", t)])
                sp.wait_ge(s_ffn3, t - 1)
                tpose(s_tpr[t % 2],
                      r1T[t % 2][:].rearrange("p (c q) -> p c q", c=9),
                      r1x[t % 2][:].rearrange("p (c q) -> p c q", c=9))

    mybir.codegen_inst_isa_subclasses(nc)
    return nc


_NC_CACHE = None


def kernel(L0_attn_x, L0_attn_Wq, L0_attn_bq, L0_attn_Wk, L0_attn_bk,
           L0_attn_Wv, L0_attn_bv, L0_ffn_packed_weight, L0_ffn_weight_scale,
           L0_ffn_bias):
    global _NC_CACHE
    _register_ops()
    if _NC_CACHE is None:
        _NC_CACHE = build_nc()
    nc = _NC_CACHE

    x = np.ascontiguousarray(np.asarray(L0_attn_x, dtype=np.float32).reshape(SEQ, D))
    com = {
        "wq": np.ascontiguousarray(np.asarray(L0_attn_Wq, np.float32)),
        "wk": np.ascontiguousarray(np.asarray(L0_attn_Wk, np.float32)),
        "wv": np.ascontiguousarray(np.asarray(L0_attn_Wv, np.float32)),
        "bq": np.ascontiguousarray(np.asarray(L0_attn_bq, np.float32)),
        "bk": np.ascontiguousarray(np.asarray(L0_attn_bk, np.float32)),
        "bv": np.ascontiguousarray(np.asarray(L0_attn_bv, np.float32)),
        "pkw": np.ascontiguousarray(np.asarray(L0_ffn_packed_weight, np.int32)),
        "wsc": np.ascontiguousarray(np.asarray(L0_ffn_weight_scale, np.float32).reshape(1)),
        "fbias": np.ascontiguousarray(np.asarray(L0_ffn_bias, np.float32)),
    }
    in_maps = [dict(com, x=np.ascontiguousarray(x[c * TOK:(c + 1) * TOK]))
               for c in range(NCORES)]
    res = run_bass_kernel_spmd(nc, in_maps, core_ids=list(range(NCORES)))
    out = np.concatenate([np.asarray(r["out"], np.float32) for r in res.results], axis=0)
    return out.reshape(1, SEQ, D)
